# version 09
# baseline (speedup 1.0000x reference)
import numpy as np
import ml_dtypes
import concourse.bass as bass
import concourse.mybir as mybir
from concourse.bass_utils import run_bass_kernel_spmd
import contextlib

F32 = mybir.dt.float32
BF16 = mybir.dt.bfloat16
AF = mybir.ActivationFunctionType
ALU = mybir.AluOpType
AX = mybir.AxisListType

B, N, D, H, HD, INTER = 4, 2048, 4096, 32, 128, 11008
QT = 8
KT = 16
DC = 32
IT = 86
NBLK = 43
EPS_LN = 1e-5
EPS_RMS = 1e-6
S128 = 1.0 / np.sqrt(128.0)


def build(debug=False, upto="C2", only=None, b_level=4, b_zero=False, b_sub="all", a1_dve_sq=False):
    _ORDER = ["pre", "A0", "A1", "A2", "C1", "B", "C2"]
    _UPTO = _ORDER.index(upto)
    _ONLY = set(only) if only is not None else None
    nc = bass.Bass("TRN2", target_bir_lowering=False, debug=False)
    scratch_kind = "ExternalOutput" if debug else "Internal"

    x_own = nc.dram_tensor("x_own", [1024, D], F32, kind="ExternalInput").ap()
    x_oth = nc.dram_tensor("x_oth", [1024, D], F32, kind="ExternalInput").ap()
    wq = nc.dram_tensor("wq", [D, D], F32, kind="ExternalInput").ap()
    wk = nc.dram_tensor("wk", [D, D], F32, kind="ExternalInput").ap()
    wv = nc.dram_tensor("wv", [D, D], F32, kind="ExternalInput").ap()
    wo = nc.dram_tensor("wo", [D, D], F32, kind="ExternalInput").ap()
    wgT = nc.dram_tensor("wgT", [D, INTER], F32, kind="ExternalInput").ap()
    wuT = nc.dram_tensor("wuT", [D, INTER], F32, kind="ExternalInput").ap()
    wdT = nc.dram_tensor("wdT", [INTER, D], F32, kind="ExternalInput").ap()
    bq_b = nc.dram_tensor("bq_b", [128, D], F32, kind="ExternalInput").ap()
    bk_b = nc.dram_tensor("bk_b", [128, D], F32, kind="ExternalInput").ap()
    bv_b = nc.dram_tensor("bv_b", [128, D], F32, kind="ExternalInput").ap()
    bo_b = nc.dram_tensor("bo_b", [128, D], F32, kind="ExternalInput").ap()
    gq_b = nc.dram_tensor("gq_b", [128, HD], F32, kind="ExternalInput").ap()
    gk_b = nc.dram_tensor("gk_b", [128, HD], F32, kind="ExternalInput").ap()
    ident_d = nc.dram_tensor("ident", [128, 128], BF16, kind="ExternalInput").ap()

    qT_d = nc.dram_tensor("qT_s", [H, HD, 1024], BF16, kind=scratch_kind).ap()
    kT_d = nc.dram_tensor("kT_s", [H, HD, 2048], BF16, kind=scratch_kind).ap()
    v_d = nc.dram_tensor("v_s", [H, KT, 128, HD], BF16, kind=scratch_kind).ap()
    hT_d = nc.dram_tensor("hT_s", [IT, 128, 1024], BF16, kind=scratch_kind).ap()
    dn_d = nc.dram_tensor("dn_s", [1024, D], F32, kind=scratch_kind).ap()
    wd_bf = nc.dram_tensor("wd_bf", [INTER, D], BF16, kind="Internal").ap()
    out = nc.dram_tensor("out", [1024, D], F32, kind="ExternalOutput").ap()
    if b_zero:
        qT_z = nc.dram_tensor("qT_z", [H, HD, 1024], BF16, kind="Internal").ap()
        kT_z = nc.dram_tensor("kT_z", [H, HD, 2048], BF16, kind="Internal").ap()
        v_z = nc.dram_tensor("v_z", [H, KT, 128, HD], BF16, kind="Internal").ap()

    est = contextlib.ExitStack()
    sem = lambda name: est.enter_context(nc.semaphore(name))
    dsem = lambda name, n: [est.enter_context(nc.semaphore(f"{name}{i}")) for i in range(n)]
    sbuf = lambda name, shape, dt: est.enter_context(nc.sbuf_tensor(name, shape, dt))

    class Chain:
        def __init__(self, s):
            self.s = s
            self.n = 0

        def done(self, instr):
            instr.then_inc(self.s, 1)
            self.n += 1

        def wait(self, eng):
            eng.wait_ge(self.s, self.n)

    with est:
        bar = sem("bar")
        ch_dve = Chain(sem("ch_dve"))
        phase_idx = [0]

        def phase(_pn=None, **emitters):
            if _pn is not None and _ORDER.index(_pn) > _UPTO:
                return
            if _ONLY is not None and _pn is not None and _pn not in _ONLY:
                return
            with nc.Block() as blk:
                for name in ("tensor", "vector", "scalar", "gpsimd", "sync"):
                    fn = emitters.get(name)

                    def sect(eng, fn=fn):
                        if phase_idx[0]:
                            eng.wait_ge(bar, 5 * phase_idx[0])
                        if fn:
                            fn(eng)
                        eng.nop().then_inc(bar, 1)

                    getattr(blk, name)(sect)
            phase_idx[0] += 1

        ident = sbuf("identb", [128, 128], BF16)
        negb = sbuf("negb", [128, 1], F32)

        def pre_sync(eng):
            eng.dma_start(out=ident[:], in_=ident_d[:, :]).then_inc(s_id, 16)
            eng.wait_ge(s_id, 16)

        s_id = sem("s_id")
        phase(_pn='pre', sync=pre_sync)

        stA = contextlib.ExitStack()
        sbufA = lambda name, shape, dt: stA.enter_context(nc.sbuf_tensor(name, shape, dt))
        xnT = sbufA("xnT", [128, DC, 2048], BF16)

        stA0 = contextlib.ExitStack()
        sb0 = lambda name, shape, dt: stA0.enter_context(nc.sbuf_tensor(name, shape, dt))
        xbuf = sb0("xbuf", [128, 2, D], F32)
        xnt = sb0("xnt", [128, 2, D], BF16)
        bn6 = sb0("bn6", [128, 2, 48], F32)
        mv = sb0("mv", [128, 2, 2], F32)
        t2b = sb0("t2b", [128, 2], F32)
        rstd0 = sb0("rstd0", [128, 2], F32)
        nb0 = sb0("nb0", [128, 2], F32)
        ptrA0 = stA0.enter_context(nc.psum_tensor("ptrA0", [128, 2, 1024], BF16))

        s_wdc = sem("s_wdc")
        s_x = dsem("s_x", 2); s_bn = sem("s_bn"); s_t2 = sem("s_t2")
        s_rstd = sem("s_rstd"); s_nb = sem("s_nb"); s_xn = sem("s_xn")
        s_tr0 = sem("s_tr0"); s_trev0 = sem("s_trev0")

        def a0_sync(eng):
            for tt in range(KT):
                if tt >= 2:
                    eng.wait_ge(s_xn, tt - 1)
                    eng.wait_ge(s_bn, tt - 1)
                src = x_own if tt < QT else x_oth
                row0 = (tt % QT) * 128
                eng.dma_start(out=xbuf[:, tt % 2, :], in_=src[row0:row0 + 128, :]).then_inc(s_x[tt % 2], 16)

        def a0_vector(eng):
            for tt in range(KT):
                p = tt % 2
                eng.wait_ge(s_x[tt % 2], 16 * (tt // 2 + 1))
                for j in range(8):
                    bs = eng.bn_stats(bn6[:, p, 6 * j:6 * j + 6], xbuf[:, p, 512 * j:512 * (j + 1)])
                ch_dve.done(bs)
                ch_dve.wait(eng)
                eng.bn_aggr(mv[:, p, :], bn6[:, p, :]).then_inc(s_bn, 1)
                eng.wait_ge(s_bn, tt + 1)
                ts0 = eng.tensor_scalar(t2b[:, p:p + 1], mv[:, p, 1:2], EPS_LN, None, ALU.add)
                ch_dve.done(ts0)
                ch_dve.wait(eng)
                eng.reciprocal(t2b[:, p:p + 1], t2b[:, p:p + 1]).then_inc(s_t2, 1)
                eng.wait_ge(s_rstd, tt + 1)
                eng.tensor_scalar(nb0[:, p:p + 1], mv[:, p, 0:1], rstd0[:, p:p + 1], -1.0,
                                  ALU.mult, ALU.mult).then_inc(s_nb, 1)
                for g in range(4):
                    gidx = tt * 4 + g
                    eng.wait_ge(s_tr0, gidx + 1)
                    eng.tensor_copy(xnT[:, 8 * g:8 * g + 8, tt * 128:(tt + 1) * 128],
                                    ptrA0[:, gidx % 2, :].rearrange("p (c q) -> p c q", c=8)
                                    ).then_inc(s_trev0, 1)

        def a0_scalar(eng):
            for tt in range(KT):
                p = tt % 2
                eng.wait_ge(s_t2, tt + 1)
                eng.activation(rstd0[:, p:p + 1], t2b[:, p:p + 1], AF.Sqrt).then_inc(s_rstd, 1)
                eng.wait_ge(s_nb, tt + 1)
                if tt >= 2:
                    eng.wait_ge(s_trev0, 4 * (tt - 1))
                eng.activation(xnt[:, p, :], xbuf[:, p, :], AF.Identity,
                               bias=nb0[:, p:p + 1], scale=rstd0[:, p:p + 1]).then_inc(s_xn, 1)

        def a0_tensor(eng):
            for tt in range(KT):
                eng.wait_ge(s_xn, tt + 1)
                for g in range(4):
                    gidx = tt * 4 + g
                    if gidx >= 2:
                        eng.wait_ge(s_trev0, gidx - 1)
                    for c in range(8):
                        dc = 8 * g + c
                        tr = eng.transpose(ptrA0[:, gidx % 2, 128 * c:128 * (c + 1)],
                                           xnt[:, tt % 2, 128 * dc:128 * (dc + 1)], ident[:])
                    tr.then_inc(s_tr0, 1)

        phase(_pn='A0', sync=a0_sync, vector=a0_vector, scalar=a0_scalar, tensor=a0_tensor)
        if debug:
            xnT_dump = nc.dram_tensor("xnT_dump", [128, DC, 2048], BF16, kind="ExternalOutput").ap()
            xnt_dump = nc.dram_tensor("xnt_dump", [128, 2, D], BF16, kind="ExternalOutput").ap()
            ln_dump = nc.dram_tensor("ln_dump", [128, 8], F32, kind="ExternalOutput").ap()
            s_dmp = sem("s_dmp")

            def dump_sync(eng):
                eng.dma_start(out=xnT_dump[:, :, :], in_=xnT[:, :, :]).then_inc(s_dmp, 16)
                eng.dma_start(out=xnt_dump[:, :, :], in_=xnt[:, :, :]).then_inc(s_dmp, 16)
                eng.dma_start(out=ln_dump[:, 0:4], in_=mv[:, :, :]).then_inc(s_dmp, 16)
                eng.dma_start(out=ln_dump[:, 4:6], in_=rstd0[:, :]).then_inc(s_dmp, 16)
                eng.dma_start(out=ln_dump[:, 6:8], in_=nb0[:, :]).then_inc(s_dmp, 16)
                eng.wait_ge(s_dmp, 16 * 5)

            phase(_pn='A0', sync=dump_sync)
        stA0.close()

        stA1 = contextlib.ExitStack()
        sb1 = lambda name, shape, dt: stA1.enter_context(nc.sbuf_tensor(name, shape, dt))
        wbuf = sb1("wbuf", [128, 2, DC, 512], BF16)
        bias_c = sb1("bias_c", [128, 2, 512], BF16)
        ptile = sb1("ptile", [128, 4, 512], BF16)
        sq_scr = sb1("sq_scr", [128, 4, 128], BF16)
        ms4 = sb1("ms4", [128, 4], F32)
        rstd4 = sb1("rstd4", [128, 2, 4], F32)
        qn = sb1("qn", [128, 2, 512], BF16)
        stg = sb1("stg", [128, 2, 512], BF16)
        vtile = sb1("vtile", [128, 2, 512], BF16)
        gq_sb = sb1("gq_sb", [128, HD], F32)
        gk_sb = sb1("gk_sb", [128, HD], F32)
        g2_b4 = sb1("g2_b4", [128, 512], BF16)
        psP = stA1.enter_context(nc.psum_tensor("psP", [128, 4, 512], F32))
        ptrA1 = stA1.enter_context(nc.psum_tensor("ptrA1", [128, 2, 1024], BF16))

        s_w1 = dsem("s_w1", 2); s_b1 = dsem("s_b1", 2); s_g1 = sem("s_g1"); s_ev1 = sem("s_ev1")
        s_sq1 = sem("s_sq1"); s_r1 = sem("s_r1"); s_rs1 = sem("s_rs1"); s_rm1 = sem("s_rm1")
        s_tr1 = sem("s_tr1"); s_te1 = sem("s_te1"); s_stg = dsem("s_stg", 2); s_vst = dsem("s_vst", 2)
        s_gld = sem("s_gld"); s_g2 = sem("s_g2")

        projs = [
            dict(name="q", w=wq, bias=bq_b, nq=QT, rms=True, sc2=S128, gains=False, dst="qT"),
            dict(name="k", w=wk, bias=bk_b, nq=KT, rms=True, sc2=1.0, gains=True, dst="kT"),
            dict(name="v", w=wv, bias=bv_b, nq=KT, rms=False, sc2=1.0, gains=False, dst="v"),
        ]
        sched = []
        chunks = []
        gidx_c = 0
        for pj in projs:
            for dout in range(8):
                chunks.append(dict(proj=pj, dout=dout, widx=len(chunks), gbefore=gidx_c))
                for t in range(pj["nq"]):
                    sched.append(dict(proj=pj, dout=dout, t=t, g=gidx_c, widx=len(chunks) - 1))
                    gidx_c += 1
        n_groups = gidx_c
        rms_groups = [e for e in sched if e["proj"]["rms"]]
        for i, e in enumerate(rms_groups):
            e["tr"] = i
        v_groups = [e for e in sched if not e["proj"]["rms"]]
        for i, e in enumerate(v_groups):
            e["vi"] = i
        N_TR_STORES = len(rms_groups)

        def a1_gpsimd(eng):
            for ch in chunks:
                if ch["widx"] >= 2:
                    prev = chunks[ch["widx"] - 2]
                    eng.wait_ge(s_g1, prev["gbefore"] + prev["proj"]["nq"])
                    eng.wait_ge(s_ev1, prev["gbefore"] + prev["proj"]["nq"])
                p = ch["widx"] % 2
                src = ch["proj"]["w"].rearrange("(c k) n -> k c n", k=128)
                eng.dma_start(out=wbuf[:, p, :, :],
                              in_=src[:, :, 512 * ch["dout"]:512 * (ch["dout"] + 1)]).then_inc(s_w1[ch["widx"] % 2], 16)
                eng.dma_start(out=bias_c[:, p, :],
                              in_=ch["proj"]["bias"][:, 512 * ch["dout"]:512 * (ch["dout"] + 1)]).then_inc(s_b1[ch["widx"] % 2], 16)

        def a1_sync(eng):
            eng.dma_start(out=gq_sb[:], in_=gq_b[:, :]).then_inc(s_gld, 16)
            eng.dma_start(out=gk_sb[:], in_=gk_b[:, :]).then_inc(s_gld, 16)

        def a1_tensor(eng):
            pend = []
            def emit_tr_one():
                e = pend.pop(0)
                eng.wait_ge(s_rm1, e["tr"] + 1)
                if e["tr"] >= 2:
                    eng.wait_ge(s_te1, e["tr"] - 1)
                for c in range(4):
                    tr = eng.transpose(ptrA1[:, e["tr"] % 2, 128 * c:128 * (c + 1)],
                                       qn[:, e["g"] % 2, 128 * c:128 * (c + 1)], ident[:])
                tr.then_inc(s_tr1, 1)
            for e in sched:
                if not e["proj"]["rms"]:
                    while pend:
                        emit_tr_one()
                eng.wait_ge(s_w1[e["widx"] % 2], 16 * (e["widx"] // 2 + 1))
                if e["g"] >= 4:
                    eng.wait_ge(s_ev1, e["g"] - 3)
                tok0 = e["t"] * 128
                for din in range(DC):
                    mm = eng.matmul(psP[:, e["g"] % 4, :], xnT[:, din, tok0:tok0 + 128],
                                    wbuf[:, e["widx"] % 2, din, :],
                                    start=(din == 0), stop=(din == DC - 1))
                mm.then_inc(s_g1, 1)
                if e["proj"]["rms"]:
                    pend.append(e)
                    if len(pend) >= 2:
                        emit_tr_one()
            while pend:
                emit_tr_one()

        def a1_vector(eng):
            eng.wait_ge(s_gld, 32)
            g2i = eng.tensor_tensor(g2_b4[:, 0:128], gq_sb[:], gk_sb[:], ALU.mult)
            ch_dve.done(g2i)
            ch_dve.wait(eng)
            for c in range(1, 4):
                cp = eng.tensor_copy(g2_b4[:, 128 * c:128 * (c + 1)], g2_b4[:, 0:128])
            eng.reduce_max(negb[:, 0:1], g2_b4[:, 0:128], axis=AX.X,
                           apply_absolute_value=True, negate=True).then_inc(s_g2, 1)
            for e in sched:
                eng.wait_ge(s_g1, e["g"] + 1)
                eng.wait_ge(s_b1[e["widx"] % 2], 16 * (e["widx"] // 2 + 1))
                bank = psP[:, e["g"] % 4, :]
                bias = bias_c[:, e["widx"] % 2, :]
                if not e["proj"]["rms"]:
                    vi = e["vi"]
                    if vi >= 2:
                        eng.wait_ge(s_vst[vi % 2], 16 * (vi // 2))
                    eng.tensor_tensor(vtile[:, vi % 2, :], bank, bias, ALU.add).then_inc(s_ev1, 1)
                    continue
                tr = e["tr"]
                eng.tensor_tensor(ptile[:, e["g"] % 4, :], bank, bias, ALU.add).then_inc(s_ev1, 1)
                pt = ptile[:, e["g"] % 4, :]
                if a1_dve_sq:
                    for c in range(4):
                        sqi = eng.tensor_tensor(sq_scr[:, c, :], pt[:, 128 * c:128 * (c + 1)],
                                                pt[:, 128 * c:128 * (c + 1)], ALU.mult)
                    ch_dve.done(sqi)
                    ch_dve.wait(eng)
                    rdi = eng.tensor_reduce(ms4[:], sq_scr[:, :, :], AX.X, ALU.add)
                    ch_dve.done(rdi)
                    ch_dve.wait(eng)
                else:
                    eng.wait_ge(s_sq1, tr + 1)
                tsm = eng.tensor_scalar(ms4[:], ms4[:], 1.0 / HD, EPS_RMS, ALU.mult, ALU.add)
                ch_dve.done(tsm)
                ch_dve.wait(eng)
                eng.reciprocal(ms4[:], ms4[:]).then_inc(s_r1, 1)
                eng.wait_ge(s_rs1, tr + 1)
                if tr >= 2:
                    eng.wait_ge(s_tr1, tr - 1)
                if e["proj"]["gains"]:
                    eng.wait_ge(s_g2, 1)
                    qsl = qn[:, e["g"] % 2, :]
                    for c in range(4):
                        kts = eng.tensor_scalar(qsl[:, 128 * c:128 * (c + 1)], pt[:, 128 * c:128 * (c + 1)],
                                                rstd4[:, tr % 2, c:c + 1], None, ALU.mult)
                    ch_dve.done(kts)
                    ch_dve.wait(eng)
                    eng.tensor_tensor(qsl, qsl, g2_b4[:], ALU.mult).then_inc(s_rm1, 1)
                else:
                    for c in range(4):
                        ts = eng.tensor_scalar(qn[:, e["g"] % 2, 128 * c:128 * (c + 1)],
                                               pt[:, 128 * c:128 * (c + 1)],
                                               rstd4[:, tr % 2, c:c + 1], e["proj"]["sc2"],
                                               ALU.mult, ALU.mult)
                    ts.then_inc(s_rm1, 1)
                eng.wait_ge(s_tr1, tr + 1)
                if tr >= 2:
                    eng.wait_ge(s_stg[tr % 2], 16 * (tr // 2))
                eng.tensor_copy(stg[:, tr % 2, :], ptrA1[:, tr % 2, 0:512]).then_inc(s_te1, 1)

        def a1_scalar(eng):
            tr = 0
            for e in sched:
                if not e["proj"]["rms"]:
                    vi = e["vi"]
                    eng.wait_ge(s_ev1, e["g"] + 1)
                    h0 = 4 * e["dout"]
                    dst = v_d[h0:h0 + 4, e["t"]].rearrange("h k d -> k h d")
                    eng.dma_start(out=dst,
                                  in_=vtile[:, vi % 2, :].rearrange("p (h d) -> p h d", h=4)
                                  ).then_inc(s_vst[vi % 2], 16)
                    continue
                if not a1_dve_sq:
                    eng.wait_ge(s_ev1, e["g"] + 1)
                    for c in range(4):
                        sq = eng.activation(sq_scr[:, c, :], ptile[:, e["g"] % 4, 128 * c:128 * (c + 1)],
                                            AF.Square, accum_out=ms4[:, c:c + 1])
                    sq.then_inc(s_sq1, 1)
                eng.wait_ge(s_r1, tr + 1)
                eng.activation(rstd4[:, tr % 2, :], ms4[:], AF.Sqrt).then_inc(s_rs1, 1)
                eng.wait_ge(s_te1, tr + 1)
                h0 = 4 * e["dout"]
                if e["proj"]["dst"] == "qT":
                    dst = qT_d[h0:h0 + 4, :, 128 * e["t"]:128 * (e["t"] + 1)].rearrange("h d q -> d h q")
                else:
                    dst = kT_d[h0:h0 + 4, :, 128 * e["t"]:128 * (e["t"] + 1)].rearrange("h d k -> d h k")
                eng.dma_start(out=dst,
                              in_=stg[:, tr % 2, :].rearrange("p (h q) -> p h q", h=4)
                              ).then_inc(s_stg[tr % 2], 16)
                tr += 1
            eng.wait_ge(s_stg[0], 16 * 96)
            eng.wait_ge(s_stg[1], 16 * 96)
            eng.wait_ge(s_vst[0], 16 * 64)
            eng.wait_ge(s_vst[1], 16 * 64)

        phase(_pn='A1', gpsimd=a1_gpsimd, sync=a1_sync, tensor=a1_tensor, vector=a1_vector, scalar=a1_scalar)
        stA1.close()

        stA2 = contextlib.ExitStack()
        sb2 = lambda name, shape, dt: stA2.enter_context(nc.sbuf_tensor(name, shape, dt))
        wgbuf = sb2("wgbuf", [128, 2, DC, 256], BF16)
        wubuf = sb2("wubuf", [128, 2, DC, 256], BF16)
        gt = sb2("gt", [128, 2, 1024], BF16)
        htile = sb2("htile", [128, 2, 1024], BF16)
        psG = stA2.enter_context(nc.psum_tensor("psG", [128, 2, 1024], F32))
        psU = stA2.enter_context(nc.psum_tensor("psU", [128, 2, 1024], F32))

        s_w2 = dsem("s_w2", 2); s_gmm = sem("s_gmm"); s_umm = sem("s_umm")
        s_sil = sem("s_sil"); s_h2 = sem("s_h2"); s_hst = dsem("s_hst", 2)

        def a2_gpsimd(eng):
            rows = INTER // 16
            for bi in range(NBLK):
                if bi >= 2:
                    eng.wait_ge(s_umm, 2 * (bi - 1))
                p = bi % 2
                sg = wgT.rearrange("(c k) n -> k c n", k=128)
                su = wuT.rearrange("(c k) n -> k c n", k=128)
                eng.dma_start(out=wgbuf[:, p, :, :], in_=sg[:, :, 256 * bi:256 * (bi + 1)]).then_inc(s_w2[p], 16)
                eng.dma_start(out=wubuf[:, p, :, :], in_=su[:, :, 256 * bi:256 * (bi + 1)]).then_inc(s_w2[p], 16)
                if 2 <= bi < 18:
                    j = bi - 2
                    eng.dma_start(out=wd_bf[j * rows:(j + 1) * rows, :],
                                  in_=wdT[j * rows:(j + 1) * rows, :],
                                  max_dma_last_dim=8192).then_inc(s_wdc, 16)

        def a2_tensor(eng):
            for bi in range(NBLK):
                eng.wait_ge(s_w2[bi % 2], 32 * (bi // 2 + 1))
                for it in range(2):
                    ii = 2 * bi + it
                    pg = ii % 2
                    if ii >= 2:
                        eng.wait_ge(s_sil, ii - 1)
                        eng.wait_ge(s_h2, ii - 1)
                    for half in range(2):
                        for hc in range(DC):
                            mm = eng.matmul(psG[:, pg, 512 * half:512 * (half + 1)],
                                            wgbuf[:, bi % 2, hc, 128 * it:128 * (it + 1)],
                                            xnT[:, hc, 512 * half:512 * (half + 1)],
                                            start=(hc == 0), stop=(hc == DC - 1))
                    mm.then_inc(s_gmm, 1)
                    for half in range(2):
                        for hc in range(DC):
                            mm = eng.matmul(psU[:, pg, 512 * half:512 * (half + 1)],
                                            wubuf[:, bi % 2, hc, 128 * it:128 * (it + 1)],
                                            xnT[:, hc, 512 * half:512 * (half + 1)],
                                            start=(hc == 0), stop=(hc == DC - 1))
                    mm.then_inc(s_umm, 1)

        def a2_scalar(eng):
            for ii in range(IT):
                eng.wait_ge(s_gmm, ii + 1)
                if ii >= 2:
                    eng.wait_ge(s_h2, ii - 1)
                eng.activation(gt[:, ii % 2, :], psG[:, ii % 2, :], AF.Silu).then_inc(s_sil, 1)

        def a2_vector(eng):
            for ii in range(IT):
                eng.wait_ge(s_sil, ii + 1)
                eng.wait_ge(s_umm, ii + 1)
                if ii >= 2:
                    eng.wait_ge(s_hst[ii % 2], 16 * (ii // 2))
                eng.tensor_tensor(htile[:, ii % 2, :], gt[:, ii % 2, :], psU[:, ii % 2, :],
                                  ALU.mult).then_inc(s_h2, 1)

        def a2_sync(eng):
            for ii in range(IT):
                eng.wait_ge(s_h2, ii + 1)
                eng.dma_start(out=hT_d[ii], in_=htile[:, ii % 2, :]).then_inc(s_hst[ii % 2], 16)
            eng.wait_ge(s_hst[0], 16 * 43)
            eng.wait_ge(s_hst[1], 16 * 43)

        phase(_pn='A2', gpsimd=a2_gpsimd, tensor=a2_tensor, scalar=a2_scalar, vector=a2_vector, sync=a2_sync)
        stA2.close()
        stA.close()

        stC1 = contextlib.ExitStack()
        sbc = lambda name, shape, dt: stC1.enter_context(nc.sbuf_tensor(name, shape, dt))
        wdbuf = sbc("wdbuf", [128, 2, IT, 512], BF16)
        hbuf = sbc("hbuf", [128, 8, 1024], BF16)
        dstage = sbc("dstage", [128, 2, 512], F32)
        bo_sb = sbc("bo_sb", [128, D], BF16)
        psD = stC1.enter_context(nc.psum_tensor("psD", [128, 8, 512], F32))

        s_wd = dsem("s_wd", 2); s_hl = dsem("s_hl", 4); s_di = sem("s_di")
        s_dev = sem("s_dev"); s_dst = dsem("s_dst", 2); s_bo = sem("s_bo")

        WD_SPLIT = [(0, 22), (22, 22), (44, 21), (65, 21)]

        def c1_gpsimd(eng):
            eng.dma_start(out=bo_sb[:], in_=bo_b[:, :], max_dma_last_dim=8192).then_inc(s_bo, 16)
            eng.wait_ge(s_wdc, 16 * 16)
            for dout in range(8):
                if dout >= 2:
                    eng.wait_ge(s_di, 86 * (dout - 1))
                p = dout % 2
                src = wd_bf.rearrange("(c k) n -> k c n", k=128)
                for (c0, cn) in WD_SPLIT:
                    eng.dma_start(out=wdbuf[:, p, c0:c0 + cn, :],
                                  in_=src[:, c0:c0 + cn, 512 * dout:512 * (dout + 1)]).then_inc(s_wd[p], 16)

        def c1_sync(eng):
            for dout in range(8):
                for p2 in range(IT // 2):
                    gp = dout * (IT // 2) + p2
                    if gp >= 4:
                        k_last = (gp - 4) * 2 + 2
                        eng.wait_ge(s_di, k_last)
                    eng.dma_start(out=hbuf[:, (gp % 4) * 2:(gp % 4) * 2 + 2, :],
                                  in_=hT_d[2 * p2:2 * p2 + 2].rearrange("i p c -> p i c")
                                  ).then_inc(s_hl[gp % 4], 16)

        def c1_tensor(eng):
            for dout in range(8):
                eng.wait_ge(s_wd[dout % 2], 64 * (dout // 2 + 1))
                for i in range(IT):
                    gp = dout * (IT // 2) + i // 2
                    eng.wait_ge(s_hl[gp % 4], 16 * (gp // 4 + 1))
                    slot = (gp % 4) * 2 + (i % 2)
                    for q in range(8):
                        if i == 0 and dout >= 1:
                            eng.wait_ge(s_dev, 8 * (dout - 1) + q + 1)
                        mm = eng.matmul(psD[:, q, :], hbuf[:, slot, 128 * q:128 * (q + 1)],
                                        wdbuf[:, dout % 2, i, :],
                                        start=(i == 0), stop=(i == IT - 1))
                    mm.then_inc(s_di, 1)

        def c1_vector(eng):
            eng.wait_ge(s_bo, 16)
            for dout in range(8):
                for q in range(8):
                    k = dout * 8 + q
                    eng.wait_ge(s_di, 86 * (dout + 1))
                    if k >= 2:
                        eng.wait_ge(s_dst[k % 2], 16 * (k // 2))
                    eng.tensor_tensor(dstage[:, k % 2, :], psD[:, q, :],
                                      bo_sb[:, 512 * dout:512 * (dout + 1)], ALU.add).then_inc(s_dev, 1)

        def c1_scalar(eng):
            for dout in range(8):
                for q in range(8):
                    k = dout * 8 + q
                    eng.wait_ge(s_dev, k + 1)
                    eng.dma_start(out=dn_d[128 * q:128 * (q + 1), 512 * dout:512 * (dout + 1)],
                                  in_=dstage[:, k % 2, :]).then_inc(s_dst[k % 2], 16)
            eng.wait_ge(s_dst[0], 16 * 32)
            eng.wait_ge(s_dst[1], 16 * 32)

        phase(_pn='C1', gpsimd=c1_gpsimd, sync=c1_sync, tensor=c1_tensor, vector=c1_vector, scalar=c1_scalar)
        stC1.close()

        stBC = contextlib.ExitStack()
        oT = stBC.enter_context(nc.sbuf_tensor("oT", [128, H, 1024], BF16))
        wobuf = stBC.enter_context(nc.sbuf_tensor("wobuf", [128, 2, DC, 512], BF16))
        xbufc = stBC.enter_context(nc.sbuf_tensor("xbufc", [128, 3, 512], F32))
        dnbufc = stBC.enter_context(nc.sbuf_tensor("dnbufc", [128, 3, 512], F32))
        s_wo = dsem("s_wo", 2)
        s_xl = dsem("s_xl", 3)

        stB = contextlib.ExitStack()
        sbb = lambda name, shape, dt: stB.enter_context(nc.sbuf_tensor(name, shape, dt))
        kbuf = sbb("kbuf", [128, 2, 2048], BF16)
        vbuf = sbb("vbuf", [128, 2, KT, 128], BF16)
        qbuf = sbb("qbuf", [128, 2, 1024], BF16)
        P = sbb("P", [128, 3, 2048], BF16)
        PT = sbb("PT", [128, KT, 1024], BF16)
        sumexp = sbb("sumexp", [128, 2, QT], F32)
        rs = sbb("rs", [128, QT], F32)
        bscr = sbb("bscr", [128, 1], F32)
        psS = stB.enter_context(nc.psum_tensor("psS", [128, 4, 512], F32))
        psO = stB.enter_context(nc.psum_tensor("psO", [128, 2, 512], F32))
        ptrB = stB.enter_context(nc.psum_tensor("ptrB", [128, 2, 1024], BF16))

        s_kl = dsem("s_kl", 2); s_smm = sem("s_smm"); s_expa = sem("s_expa"); s_expb = sem("s_expb")
        s_rc = sem("s_rc"); s_nrm = sem("s_nrm"); s_ptr = sem("s_ptr")
        s_pte0 = sem("s_pte0"); s_pte1 = sem("s_pte1")
        s_av = sem("s_av"); s_oev = sem("s_oev")

        def b_sync(eng):
            for h in range(H):
                if h >= 2:
                    eng.wait_ge(s_av, h - 1)
                p = h % 2
                eng.dma_start(out=kbuf[:, p, :], in_=kT_d[h]).then_inc(s_kl[p], 16)
                eng.dma_start(out=vbuf[:, p, :, :],
                              in_=v_d[h].rearrange("t k d -> k t d")).then_inc(s_kl[p], 16)
                eng.dma_start(out=qbuf[:, p, :], in_=qT_d[h]).then_inc(s_kl[p], 16)
            for idx in range(3):
                dout, q = idx // 8, idx % 8
                rsl = slice(128 * q, 128 * (q + 1))
                csl = slice(512 * dout, 512 * (dout + 1))
                eng.dma_start(out=xbufc[:, idx % 3, :], in_=x_own[rsl, csl]).then_inc(s_xl[idx % 3], 16)
                eng.dma_start(out=dnbufc[:, idx % 3, :], in_=dn_d[rsl, csl]).then_inc(s_xl[idx % 3], 16)

        def b_tensor(eng):
            for h in range(H):
                eng.wait_ge(s_kl[h % 2], 48 * (h // 2 + 1))
                for qt in range(QT):
                    step = QT * h + qt
                    eng.wait_ge(s_expa, step)
                    for j in range(2):
                        mm = eng.matmul(psS[:, j, :], qbuf[:, h % 2, 128 * qt:128 * (qt + 1)],
                                        kbuf[:, h % 2, 512 * j:512 * (j + 1)],
                                        start=True, stop=True)
                    mm.then_inc(s_smm, 1)
                    eng.wait_ge(s_expb, step)
                    for j in range(2, 4):
                        mm = eng.matmul(psS[:, j, :], qbuf[:, h % 2, 128 * qt:128 * (qt + 1)],
                                        kbuf[:, h % 2, 512 * j:512 * (j + 1)],
                                        start=True, stop=True)
                    mm.then_inc(s_smm, 1)
                    if qt >= 2:
                        emit_ptr(eng, h, qt - 2)
                emit_ptr(eng, h, QT - 2)
                emit_ptr(eng, h, QT - 1)
                eng.wait_ge(s_pte0, 8 * (h + 1))
                eng.wait_ge(s_pte1, 8 * (h + 1))
                if h >= 1:
                    eng.wait_ge(s_oev, h)
                for kc in range(KT):
                    eng.matmul(psO[:, 0, :], vbuf[:, h % 2, kc, :], PT[:, kc, 0:512],
                               start=(kc == 0), stop=(kc == KT - 1))
                    mm = eng.matmul(psO[:, 1, :], vbuf[:, h % 2, kc, :], PT[:, kc, 512:1024],
                                    start=(kc == 0), stop=(kc == KT - 1))
                mm.then_inc(s_av, 1)

        def emit_ptr(eng, h, qtx):
            step = QT * h + qtx
            eng.wait_ge(s_nrm, step + 1)
            for grp in range(2):
                gi = 2 * step + grp
                if gi >= 2:
                    eng.wait_ge(s_pte0 if grp == 0 else s_pte1, gi // 2)
                for c in range(8):
                    kc = 8 * grp + c
                    tr = eng.transpose(ptrB[:, grp, 128 * c:128 * (c + 1)],
                                       P[:, qtx % 3, 128 * kc:128 * (kc + 1)], ident[:])
                tr.then_inc(s_ptr, 1)

        def b_evac(eng, h, qtx):
            gi = 2 * (QT * h + qtx)
            eng.wait_ge(s_ptr, gi + 1)
            if h >= 1:
                eng.wait_ge(s_av, h)
            eng.tensor_copy(PT[:, 0:8, 128 * qtx:128 * (qtx + 1)],
                            ptrB[:, 0, :].rearrange("p (c q) -> p c q", c=8)).then_inc(s_pte0, 1)
            eng.wait_ge(s_ptr, gi + 2)
            eng.tensor_copy(PT[:, 8:16, 128 * qtx:128 * (qtx + 1)],
                            ptrB[:, 1, :].rearrange("p (c q) -> p c q", c=8)).then_inc(s_pte1, 1)

        def b_vector(eng):
            for h in range(H):
                for qt in range(QT):
                    step = QT * h + qt
                    eng.wait_ge(s_expa, step + 1)
                    eng.wait_ge(s_expb, step + 1)
                    if h >= 1:
                        eng.wait_ge(s_nrm, QT * (h - 1) + qt + 1)
                    tse = eng.tensor_scalar(rs[:, qt:qt + 1], sumexp[:, 0, qt:qt + 1],
                                            sumexp[:, 1, qt:qt + 1], None, ALU.add)
                    ch_dve.done(tse)
                    ch_dve.wait(eng)
                    eng.reciprocal(rs[:, qt:qt + 1], rs[:, qt:qt + 1]).then_inc(s_rc, 1)
                    if qt >= 2:
                        b_evac(eng, h, qt - 2)
                b_evac(eng, h, QT - 2)
                b_evac(eng, h, QT - 1)
                eng.wait_ge(s_av, h + 1)
                eng.tensor_copy(oT[:, h, :], psO[:, :, :].rearrange("p a b -> p (a b)")).then_inc(s_oev, 1)

        def b_scalar(eng):
            for h in range(H):
                for qt in range(QT):
                    step = QT * h + qt
                    eng.wait_ge(s_smm, 2 * step + 1)
                    if step >= 3:
                        eng.wait_ge(s_ptr, 2 * (step - 3) + 2)
                    if h >= 1:
                        eng.wait_ge(s_rc, QT * (h - 1) + qt + 1)
                    eng.activation(P[:, qt % 3, 0:1024],
                                   psS[:, 0:2, :].rearrange("p a b -> p (a b)"),
                                   AF.Exp, bias=negb[:, 0:1], scale=1.0,
                                   accum_out=sumexp[:, 0, qt:qt + 1]).then_inc(s_expa, 1)
                    eng.wait_ge(s_smm, 2 * step + 2)
                    eng.activation(P[:, qt % 3, 1024:2048],
                                   psS[:, 2:4, :].rearrange("p a b -> p (a b)"),
                                   AF.Exp, bias=negb[:, 0:1], scale=1.0,
                                   accum_out=sumexp[:, 1, qt:qt + 1]).then_inc(s_expb, 1)

        def b_gpsimd(eng):
            for dout in range(2):
                srcw = wo.rearrange("(c k) n -> k c n", k=128)
                eng.dma_start(out=wobuf[:, dout % 2, :, :],
                              in_=srcw[:, :, 512 * dout:512 * (dout + 1)]).then_inc(s_wo[dout % 2], 16)
            for h in range(H):
                for qt in range(QT):
                    step = QT * h + qt
                    eng.wait_ge(s_expa, step + 1)
                    eng.wait_ge(s_expb, step + 1)
                    eng.wait_ge(s_rc, step + 1)
                    eng.tensor_scalar(P[:, qt % 3, :], P[:, qt % 3, :], rs[:, qt:qt + 1], 1.0,
                                      ALU.mult, ALU.mult).then_inc(s_nrm, 1)

        phase(_pn='B', sync=b_sync, tensor=b_tensor, vector=b_vector, scalar=b_scalar, gpsimd=b_gpsimd)
        stB.close()

        stC2 = contextlib.ExitStack()
        sbo = lambda name, shape, dt: stC2.enter_context(nc.sbuf_tensor(name, shape, dt))

        tstage = sbo("tstage", [128, 2, 512], F32)
        ostage = sbo("ostage", [128, 2, 512], F32)
        psC2 = stC2.enter_context(nc.psum_tensor("psC2", [128, 8, 512], F32))

        s_fmm = sem("s_fmm")
        s_fev = sem("s_fev"); s_f1 = sem("s_f1"); s_out = dsem("s_out", 2)

        def c2_gpsimd(eng):
            for dout in range(2, 8):
                if dout >= 2:
                    eng.wait_ge(s_fmm, 8 * (dout - 1))
                src = wo.rearrange("(c k) n -> k c n", k=128)
                eng.dma_start(out=wobuf[:, dout % 2, :, :],
                              in_=src[:, :, 512 * dout:512 * (dout + 1)]).then_inc(s_wo[dout % 2], 16)

        def c2_sync(eng):
            for dout in range(8):
                for q in range(8):
                    idx = 8 * dout + q
                    if idx < 3:
                        continue
                    eng.wait_ge(s_f1, idx - 2)
                    rsl = slice(128 * q, 128 * (q + 1))
                    csl = slice(512 * dout, 512 * (dout + 1))
                    eng.dma_start(out=xbufc[:, idx % 3, :], in_=x_own[rsl, csl]).then_inc(s_xl[idx % 3], 16)
                    eng.dma_start(out=dnbufc[:, idx % 3, :], in_=dn_d[rsl, csl]).then_inc(s_xl[idx % 3], 16)
            eng.wait_ge(s_out[0], 16 * 32)
            eng.wait_ge(s_out[1], 16 * 32)

        def c2_tensor(eng):
            for dout in range(8):
                eng.wait_ge(s_wo[dout % 2], 16 * (dout // 2 + 1))
                for q in range(8):
                    idx = 8 * dout + q
                    if idx >= 8:
                        eng.wait_ge(s_fev, idx - 7)
                    for dc in range(DC):
                        mm = eng.matmul(psC2[:, q, :], oT[:, dc, 128 * q:128 * (q + 1)],
                                        wobuf[:, dout % 2, dc, :],
                                        start=(dc == 0), stop=(dc == DC - 1))
                    mm.then_inc(s_fmm, 1)

        def c2_vector(eng):
            for dout in range(8):
                for q in range(8):
                    idx = 8 * dout + q
                    eng.wait_ge(s_fmm, idx + 1)
                    eng.wait_ge(s_xl[idx % 3], 32 * (idx // 3 + 1))
                    if idx >= 2:
                        eng.wait_ge(s_out[idx % 2], 16 * (idx // 2))
                    eng.tensor_tensor(tstage[:, idx % 2, :], psC2[:, q, :],
                                      xbufc[:, idx % 3, :], ALU.add).then_inc(s_fev, 1)
                    eng.wait_ge(s_fev, idx + 1)
                    eng.tensor_tensor(ostage[:, idx % 2, :], tstage[:, idx % 2, :],
                                      dnbufc[:, idx % 3, :], ALU.add).then_inc(s_f1, 1)

        def c2_scalar(eng):
            for dout in range(8):
                for q in range(8):
                    idx = 8 * dout + q
                    eng.wait_ge(s_f1, idx + 1)
                    eng.dma_start(out=out[128 * q:128 * (q + 1), 512 * dout:512 * (dout + 1)],
                                  in_=ostage[:, idx % 2, :]).then_inc(s_out[idx % 2], 16)
            eng.wait_ge(s_out[0], 16 * 32)
            eng.wait_ge(s_out[1], 16 * 32)

        phase(_pn='C2', gpsimd=c2_gpsimd, sync=c2_sync, tensor=c2_tensor, vector=c2_vector, scalar=c2_scalar)
        stC2.close()
        stBC.close()

    return nc


def kernel(**inp):
    x = np.ascontiguousarray(np.asarray(inp["L0_attn_hidden_states"], dtype=np.float32))
    ident = np.eye(128, dtype=ml_dtypes.bfloat16)
    f32c = lambda a: np.ascontiguousarray(np.asarray(a, dtype=np.float32))
    common = {
        "wq": f32c(inp["L0_attn_Wq"]), "wk": f32c(inp["L0_attn_Wk"]),
        "wv": f32c(inp["L0_attn_Wv"]), "wo": f32c(inp["L0_attn_Wo"]),
        "wgT": f32c(np.asarray(inp["L0_ffn_w_gate"]).T),
        "wuT": f32c(np.asarray(inp["L0_ffn_w_up"]).T),
        "wdT": f32c(np.asarray(inp["L0_ffn_w_down"]).T),
        "bq_b": f32c(np.broadcast_to(inp["L0_attn_bq"], (128, D))),
        "bk_b": f32c(np.broadcast_to(inp["L0_attn_bk"], (128, D))),
        "bv_b": f32c(np.broadcast_to(inp["L0_attn_bv"], (128, D))),
        "bo_b": f32c(np.broadcast_to(inp["L0_attn_bo"], (128, D))),
        "gq_b": f32c(np.broadcast_to(inp["L0_attn_gq"], (128, HD))),
        "gk_b": f32c(np.broadcast_to(inp["L0_attn_gk"], (128, HD))),
        "ident": ident,
    }
    in_maps = []
    for c in range(8):
        b, half = c // 2, c % 2
        xb = x[b]
        m = dict(common)
        m["x_own"] = np.ascontiguousarray(xb[half * 1024:(half + 1) * 1024])
        m["x_oth"] = np.ascontiguousarray(xb[(1 - half) * 1024:(2 - half) * 1024])
        in_maps.append(m)
    nc = build()
    res = run_bass_kernel_spmd(nc, in_maps, core_ids=list(range(8)))
    outp = np.empty((B, N, D), np.float32)
    for c, r in enumerate(res.results):
        b, half = c // 2, c % 2
        outp[b, half * 1024:(half + 1) * 1024] = r["out"]
    return outp


# version 10
# speedup vs baseline: 1.0020x; 1.0020x over previous
import numpy as np
import ml_dtypes
import concourse.bass as bass
import concourse.mybir as mybir
from concourse.bass_utils import run_bass_kernel_spmd
import contextlib

F32 = mybir.dt.float32
BF16 = mybir.dt.bfloat16
AF = mybir.ActivationFunctionType
ALU = mybir.AluOpType
AX = mybir.AxisListType

B, N, D, H, HD, INTER = 4, 2048, 4096, 32, 128, 11008
QT = 8
KT = 16
DC = 32
IT = 86
NBLK = 43
EPS_LN = 1e-5
EPS_RMS = 1e-6
S128 = 1.0 / np.sqrt(128.0)


def build(debug=False, upto="C2", only=None, b_level=4, b_zero=False, b_sub="all", a1_dve_sq=False):
    _ORDER = ["pre", "A0", "A1", "A2", "C1", "B", "C2"]
    _UPTO = _ORDER.index(upto)
    _ONLY = set(only) if only is not None else None
    nc = bass.Bass("TRN2", target_bir_lowering=False, debug=False)
    scratch_kind = "ExternalOutput" if debug else "Internal"

    x_own = nc.dram_tensor("x_own", [1024, D], F32, kind="ExternalInput").ap()
    x_oth = nc.dram_tensor("x_oth", [1024, D], F32, kind="ExternalInput").ap()
    wq = nc.dram_tensor("wq", [D, D], F32, kind="ExternalInput").ap()
    wk = nc.dram_tensor("wk", [D, D], F32, kind="ExternalInput").ap()
    wv = nc.dram_tensor("wv", [D, D], F32, kind="ExternalInput").ap()
    wo = nc.dram_tensor("wo", [D, D], F32, kind="ExternalInput").ap()
    wgT = nc.dram_tensor("wgT", [D, INTER], F32, kind="ExternalInput").ap()
    wuT = nc.dram_tensor("wuT", [D, INTER], F32, kind="ExternalInput").ap()
    wdT = nc.dram_tensor("wdT", [INTER, D], F32, kind="ExternalInput").ap()
    bq_b = nc.dram_tensor("bq_b", [128, D], F32, kind="ExternalInput").ap()
    bk_b = nc.dram_tensor("bk_b", [128, D], F32, kind="ExternalInput").ap()
    bv_b = nc.dram_tensor("bv_b", [128, D], F32, kind="ExternalInput").ap()
    bo_b = nc.dram_tensor("bo_b", [128, D], F32, kind="ExternalInput").ap()
    gq_b = nc.dram_tensor("gq_b", [128, HD], F32, kind="ExternalInput").ap()
    gk_b = nc.dram_tensor("gk_b", [128, HD], F32, kind="ExternalInput").ap()
    ident_d = nc.dram_tensor("ident", [128, 128], BF16, kind="ExternalInput").ap()

    qT_d = nc.dram_tensor("qT_s", [H, HD, 1024], BF16, kind=scratch_kind).ap()
    kT_d = nc.dram_tensor("kT_s", [H, HD, 2048], BF16, kind=scratch_kind).ap()
    v_d = nc.dram_tensor("v_s", [H, KT, 128, HD], BF16, kind=scratch_kind).ap()
    hT_d = nc.dram_tensor("hT_s", [IT, 128, 1024], BF16, kind=scratch_kind).ap()
    dn_d = nc.dram_tensor("dn_s", [1024, D], F32, kind=scratch_kind).ap()
    wd_bf = nc.dram_tensor("wd_bf", [INTER, D], BF16, kind="Internal").ap()
    out = nc.dram_tensor("out", [1024, D], F32, kind="ExternalOutput").ap()
    if b_zero:
        qT_z = nc.dram_tensor("qT_z", [H, HD, 1024], BF16, kind="Internal").ap()
        kT_z = nc.dram_tensor("kT_z", [H, HD, 2048], BF16, kind="Internal").ap()
        v_z = nc.dram_tensor("v_z", [H, KT, 128, HD], BF16, kind="Internal").ap()

    est = contextlib.ExitStack()
    sem = lambda name: est.enter_context(nc.semaphore(name))
    dsem = lambda name, n: [est.enter_context(nc.semaphore(f"{name}{i}")) for i in range(n)]
    sbuf = lambda name, shape, dt: est.enter_context(nc.sbuf_tensor(name, shape, dt))

    class Chain:
        def __init__(self, s):
            self.s = s
            self.n = 0

        def done(self, instr):
            instr.then_inc(self.s, 1)
            self.n += 1

        def wait(self, eng):
            eng.wait_ge(self.s, self.n)

    with est:
        bar = sem("bar")
        ch_dve = Chain(sem("ch_dve"))
        phase_idx = [0]

        def phase(_pn=None, **emitters):
            if _pn is not None and _ORDER.index(_pn) > _UPTO:
                return
            if _ONLY is not None and _pn is not None and _pn not in _ONLY:
                return
            with nc.Block() as blk:
                for name in ("tensor", "vector", "scalar", "gpsimd", "sync"):
                    fn = emitters.get(name)

                    def sect(eng, fn=fn):
                        if phase_idx[0]:
                            eng.wait_ge(bar, 5 * phase_idx[0])
                        if fn:
                            fn(eng)
                        eng.nop().then_inc(bar, 1)

                    getattr(blk, name)(sect)
            phase_idx[0] += 1

        ident = sbuf("identb", [128, 128], BF16)
        negb = sbuf("negb", [128, 1], F32)

        def pre_sync(eng):
            eng.dma_start(out=ident[:], in_=ident_d[:, :]).then_inc(s_id, 16)
            eng.wait_ge(s_id, 16)

        s_id = sem("s_id")
        phase(_pn='pre', sync=pre_sync)

        stA = contextlib.ExitStack()
        sbufA = lambda name, shape, dt: stA.enter_context(nc.sbuf_tensor(name, shape, dt))
        xnT = sbufA("xnT", [128, DC, 2048], BF16)

        stA0 = contextlib.ExitStack()
        sb0 = lambda name, shape, dt: stA0.enter_context(nc.sbuf_tensor(name, shape, dt))
        xbuf = sb0("xbuf", [128, 2, D], F32)
        xnt = sb0("xnt", [128, 2, D], BF16)
        bn6 = sb0("bn6", [128, 2, 48], F32)
        mv = sb0("mv", [128, 2, 2], F32)
        t2b = sb0("t2b", [128, 2], F32)
        rstd0 = sb0("rstd0", [128, 2], F32)
        nb0 = sb0("nb0", [128, 2], F32)
        ptrA0 = stA0.enter_context(nc.psum_tensor("ptrA0", [128, 2, 1024], BF16))

        s_wdc = sem("s_wdc")
        s_x = dsem("s_x", 2); s_bn = sem("s_bn"); s_t2 = sem("s_t2")
        s_rstd = sem("s_rstd"); s_nb = sem("s_nb"); s_xn = sem("s_xn")
        s_tr0 = sem("s_tr0"); s_trev0 = sem("s_trev0")

        def a0_sync(eng):
            for tt in range(KT):
                if tt >= 2:
                    eng.wait_ge(s_xn, tt - 1)
                    eng.wait_ge(s_bn, tt - 1)
                src = x_own if tt < QT else x_oth
                row0 = (tt % QT) * 128
                eng.dma_start(out=xbuf[:, tt % 2, :], in_=src[row0:row0 + 128, :]).then_inc(s_x[tt % 2], 16)

        def a0_vector(eng):
            for tt in range(KT):
                p = tt % 2
                eng.wait_ge(s_x[tt % 2], 16 * (tt // 2 + 1))
                for j in range(8):
                    bs = eng.bn_stats(bn6[:, p, 6 * j:6 * j + 6], xbuf[:, p, 512 * j:512 * (j + 1)])
                ch_dve.done(bs)
                ch_dve.wait(eng)
                eng.bn_aggr(mv[:, p, :], bn6[:, p, :]).then_inc(s_bn, 1)
                eng.wait_ge(s_bn, tt + 1)
                ts0 = eng.tensor_scalar(t2b[:, p:p + 1], mv[:, p, 1:2], EPS_LN, None, ALU.add)
                ch_dve.done(ts0)
                ch_dve.wait(eng)
                eng.reciprocal(t2b[:, p:p + 1], t2b[:, p:p + 1]).then_inc(s_t2, 1)
                eng.wait_ge(s_rstd, tt + 1)
                eng.tensor_scalar(nb0[:, p:p + 1], mv[:, p, 0:1], rstd0[:, p:p + 1], -1.0,
                                  ALU.mult, ALU.mult).then_inc(s_nb, 1)
                for g in range(4):
                    gidx = tt * 4 + g
                    eng.wait_ge(s_tr0, gidx + 1)
                    eng.tensor_copy(xnT[:, 8 * g:8 * g + 8, tt * 128:(tt + 1) * 128],
                                    ptrA0[:, gidx % 2, :].rearrange("p (c q) -> p c q", c=8)
                                    ).then_inc(s_trev0, 1)

        def a0_scalar(eng):
            for tt in range(KT):
                p = tt % 2
                eng.wait_ge(s_t2, tt + 1)
                eng.activation(rstd0[:, p:p + 1], t2b[:, p:p + 1], AF.Sqrt).then_inc(s_rstd, 1)
                eng.wait_ge(s_nb, tt + 1)
                if tt >= 2:
                    eng.wait_ge(s_trev0, 4 * (tt - 1))
                eng.activation(xnt[:, p, :], xbuf[:, p, :], AF.Identity,
                               bias=nb0[:, p:p + 1], scale=rstd0[:, p:p + 1]).then_inc(s_xn, 1)

        def a0_tensor(eng):
            for tt in range(KT):
                eng.wait_ge(s_xn, tt + 1)
                for g in range(4):
                    gidx = tt * 4 + g
                    if gidx >= 2:
                        eng.wait_ge(s_trev0, gidx - 1)
                    for c in range(8):
                        dc = 8 * g + c
                        tr = eng.transpose(ptrA0[:, gidx % 2, 128 * c:128 * (c + 1)],
                                           xnt[:, tt % 2, 128 * dc:128 * (dc + 1)], ident[:])
                    tr.then_inc(s_tr0, 1)

        phase(_pn='A0', sync=a0_sync, vector=a0_vector, scalar=a0_scalar, tensor=a0_tensor)
        if debug:
            xnT_dump = nc.dram_tensor("xnT_dump", [128, DC, 2048], BF16, kind="ExternalOutput").ap()
            xnt_dump = nc.dram_tensor("xnt_dump", [128, 2, D], BF16, kind="ExternalOutput").ap()
            ln_dump = nc.dram_tensor("ln_dump", [128, 8], F32, kind="ExternalOutput").ap()
            s_dmp = sem("s_dmp")

            def dump_sync(eng):
                eng.dma_start(out=xnT_dump[:, :, :], in_=xnT[:, :, :]).then_inc(s_dmp, 16)
                eng.dma_start(out=xnt_dump[:, :, :], in_=xnt[:, :, :]).then_inc(s_dmp, 16)
                eng.dma_start(out=ln_dump[:, 0:4], in_=mv[:, :, :]).then_inc(s_dmp, 16)
                eng.dma_start(out=ln_dump[:, 4:6], in_=rstd0[:, :]).then_inc(s_dmp, 16)
                eng.dma_start(out=ln_dump[:, 6:8], in_=nb0[:, :]).then_inc(s_dmp, 16)
                eng.wait_ge(s_dmp, 16 * 5)

            phase(_pn='A0', sync=dump_sync)
        stA0.close()

        stA1 = contextlib.ExitStack()
        sb1 = lambda name, shape, dt: stA1.enter_context(nc.sbuf_tensor(name, shape, dt))
        wbuf = sb1("wbuf", [128, 2, DC, 512], BF16)
        bias_c = sb1("bias_c", [128, 2, 512], BF16)
        ptile = sb1("ptile", [128, 4, 512], BF16)
        sq_scr = sb1("sq_scr", [128, 4, 128], BF16)
        ms4 = sb1("ms4", [128, 4], F32)
        rstd4 = sb1("rstd4", [128, 2, 4], F32)
        qn = sb1("qn", [128, 2, 512], BF16)
        stg = sb1("stg", [128, 2, 512], BF16)
        vtile = sb1("vtile", [128, 2, 512], BF16)
        gq_sb = sb1("gq_sb", [128, HD], F32)
        gk_sb = sb1("gk_sb", [128, HD], F32)
        g2_b4 = sb1("g2_b4", [128, 512], BF16)
        psP = stA1.enter_context(nc.psum_tensor("psP", [128, 4, 512], F32))
        ptrA1 = stA1.enter_context(nc.psum_tensor("ptrA1", [128, 2, 1024], BF16))

        s_w1 = dsem("s_w1", 2); s_b1 = dsem("s_b1", 2); s_g1 = sem("s_g1"); s_ev1 = sem("s_ev1")
        s_sq1 = sem("s_sq1"); s_r1 = sem("s_r1"); s_rs1 = sem("s_rs1"); s_rm1 = sem("s_rm1")
        s_tr1 = sem("s_tr1"); s_te1 = sem("s_te1"); s_stg = dsem("s_stg", 2); s_vst = dsem("s_vst", 2)
        s_gld = sem("s_gld"); s_g2 = sem("s_g2")

        projs = [
            dict(name="q", w=wq, bias=bq_b, nq=QT, rms=True, sc2=S128, gains=False, dst="qT"),
            dict(name="k", w=wk, bias=bk_b, nq=KT, rms=True, sc2=1.0, gains=True, dst="kT"),
            dict(name="v", w=wv, bias=bv_b, nq=KT, rms=False, sc2=1.0, gains=False, dst="v"),
        ]
        sched = []
        chunks = []
        gidx_c = 0
        for pj in projs:
            for dout in range(8):
                chunks.append(dict(proj=pj, dout=dout, widx=len(chunks), gbefore=gidx_c))
                for t in range(pj["nq"]):
                    sched.append(dict(proj=pj, dout=dout, t=t, g=gidx_c, widx=len(chunks) - 1))
                    gidx_c += 1
        n_groups = gidx_c
        rms_groups = [e for e in sched if e["proj"]["rms"]]
        for i, e in enumerate(rms_groups):
            e["tr"] = i
        v_groups = [e for e in sched if not e["proj"]["rms"]]
        for i, e in enumerate(v_groups):
            e["vi"] = i
        N_TR_STORES = len(rms_groups)

        def a1_gpsimd(eng):
            for ch in chunks:
                if ch["widx"] >= 2:
                    prev = chunks[ch["widx"] - 2]
                    eng.wait_ge(s_g1, prev["gbefore"] + prev["proj"]["nq"])
                    eng.wait_ge(s_ev1, prev["gbefore"] + prev["proj"]["nq"])
                p = ch["widx"] % 2
                src = ch["proj"]["w"].rearrange("(c k) n -> k c n", k=128)
                eng.dma_start(out=wbuf[:, p, :, :],
                              in_=src[:, :, 512 * ch["dout"]:512 * (ch["dout"] + 1)]).then_inc(s_w1[ch["widx"] % 2], 16)
                eng.dma_start(out=bias_c[:, p, :],
                              in_=ch["proj"]["bias"][:, 512 * ch["dout"]:512 * (ch["dout"] + 1)]).then_inc(s_b1[ch["widx"] % 2], 16)

        def a1_sync(eng):
            eng.dma_start(out=gq_sb[:], in_=gq_b[:, :]).then_inc(s_gld, 16)
            eng.dma_start(out=gk_sb[:], in_=gk_b[:, :]).then_inc(s_gld, 16)

        def a1_tensor(eng):
            pend = []
            def emit_tr_one():
                e = pend.pop(0)
                eng.wait_ge(s_rm1, e["tr"] + 1)
                if e["tr"] >= 2:
                    eng.wait_ge(s_te1, e["tr"] - 1)
                for c in range(4):
                    tr = eng.transpose(ptrA1[:, e["tr"] % 2, 128 * c:128 * (c + 1)],
                                       qn[:, e["g"] % 2, 128 * c:128 * (c + 1)], ident[:])
                tr.then_inc(s_tr1, 1)
            for e in sched:
                if not e["proj"]["rms"]:
                    while pend:
                        emit_tr_one()
                eng.wait_ge(s_w1[e["widx"] % 2], 16 * (e["widx"] // 2 + 1))
                if e["g"] >= 4:
                    eng.wait_ge(s_ev1, e["g"] - 3)
                tok0 = e["t"] * 128
                for din in range(DC):
                    mm = eng.matmul(psP[:, e["g"] % 4, :], xnT[:, din, tok0:tok0 + 128],
                                    wbuf[:, e["widx"] % 2, din, :],
                                    start=(din == 0), stop=(din == DC - 1))
                mm.then_inc(s_g1, 1)
                if e["proj"]["rms"]:
                    pend.append(e)
                    if len(pend) >= 2:
                        emit_tr_one()
            while pend:
                emit_tr_one()

        def a1_vector(eng):
            eng.wait_ge(s_gld, 32)
            g2i = eng.tensor_tensor(g2_b4[:, 0:128], gq_sb[:], gk_sb[:], ALU.mult)
            ch_dve.done(g2i)
            ch_dve.wait(eng)
            for c in range(1, 4):
                cp = eng.tensor_copy(g2_b4[:, 128 * c:128 * (c + 1)], g2_b4[:, 0:128])
            eng.reduce_max(negb[:, 0:1], g2_b4[:, 0:128], axis=AX.X,
                           apply_absolute_value=True, negate=True).then_inc(s_g2, 1)
            for e in sched:
                eng.wait_ge(s_g1, e["g"] + 1)
                eng.wait_ge(s_b1[e["widx"] % 2], 16 * (e["widx"] // 2 + 1))
                bank = psP[:, e["g"] % 4, :]
                bias = bias_c[:, e["widx"] % 2, :]
                if not e["proj"]["rms"]:
                    vi = e["vi"]
                    if vi >= 2:
                        eng.wait_ge(s_vst[vi % 2], 16 * (vi // 2))
                    eng.tensor_tensor(vtile[:, vi % 2, :], bank, bias, ALU.add).then_inc(s_ev1, 1)
                    continue
                tr = e["tr"]
                eng.tensor_tensor(ptile[:, e["g"] % 4, :], bank, bias, ALU.add).then_inc(s_ev1, 1)
                pt = ptile[:, e["g"] % 4, :]
                if a1_dve_sq:
                    for c in range(4):
                        sqi = eng.tensor_tensor(sq_scr[:, c, :], pt[:, 128 * c:128 * (c + 1)],
                                                pt[:, 128 * c:128 * (c + 1)], ALU.mult)
                    ch_dve.done(sqi)
                    ch_dve.wait(eng)
                    rdi = eng.tensor_reduce(ms4[:], sq_scr[:, :, :], AX.X, ALU.add)
                    ch_dve.done(rdi)
                    ch_dve.wait(eng)
                else:
                    eng.wait_ge(s_sq1, tr + 1)
                tsm = eng.tensor_scalar(ms4[:], ms4[:], 1.0 / HD, EPS_RMS, ALU.mult, ALU.add)
                ch_dve.done(tsm)
                ch_dve.wait(eng)
                eng.reciprocal(ms4[:], ms4[:]).then_inc(s_r1, 1)
                eng.wait_ge(s_rs1, tr + 1)
                if tr >= 2:
                    eng.wait_ge(s_tr1, tr - 1)
                if e["proj"]["gains"]:
                    eng.wait_ge(s_g2, 1)
                    qsl = qn[:, e["g"] % 2, :]
                    for c in range(4):
                        kts = eng.tensor_scalar(qsl[:, 128 * c:128 * (c + 1)], pt[:, 128 * c:128 * (c + 1)],
                                                rstd4[:, tr % 2, c:c + 1], None, ALU.mult)
                    ch_dve.done(kts)
                    ch_dve.wait(eng)
                    eng.tensor_tensor(qsl, qsl, g2_b4[:], ALU.mult).then_inc(s_rm1, 1)
                else:
                    for c in range(4):
                        ts = eng.tensor_scalar(qn[:, e["g"] % 2, 128 * c:128 * (c + 1)],
                                               pt[:, 128 * c:128 * (c + 1)],
                                               rstd4[:, tr % 2, c:c + 1], e["proj"]["sc2"],
                                               ALU.mult, ALU.mult)
                    ts.then_inc(s_rm1, 1)
                eng.wait_ge(s_tr1, tr + 1)
                if tr >= 2:
                    eng.wait_ge(s_stg[tr % 2], 16 * (tr // 2))
                eng.tensor_copy(stg[:, tr % 2, :], ptrA1[:, tr % 2, 0:512]).then_inc(s_te1, 1)

        def a1_scalar(eng):
            tr = 0
            for e in sched:
                if not e["proj"]["rms"]:
                    vi = e["vi"]
                    eng.wait_ge(s_ev1, e["g"] + 1)
                    h0 = 4 * e["dout"]
                    dst = v_d[h0:h0 + 4, e["t"]].rearrange("h k d -> k h d")
                    eng.dma_start(out=dst,
                                  in_=vtile[:, vi % 2, :].rearrange("p (h d) -> p h d", h=4)
                                  ).then_inc(s_vst[vi % 2], 16)
                    continue
                if not a1_dve_sq:
                    eng.wait_ge(s_ev1, e["g"] + 1)
                    for c in range(4):
                        sq = eng.activation(sq_scr[:, c, :], ptile[:, e["g"] % 4, 128 * c:128 * (c + 1)],
                                            AF.Square, accum_out=ms4[:, c:c + 1])
                    sq.then_inc(s_sq1, 1)
                eng.wait_ge(s_r1, tr + 1)
                eng.activation(rstd4[:, tr % 2, :], ms4[:], AF.Sqrt).then_inc(s_rs1, 1)
                eng.wait_ge(s_te1, tr + 1)
                h0 = 4 * e["dout"]
                if e["proj"]["dst"] == "qT":
                    dst = qT_d[h0:h0 + 4, :, 128 * e["t"]:128 * (e["t"] + 1)].rearrange("h d q -> d h q")
                else:
                    dst = kT_d[h0:h0 + 4, :, 128 * e["t"]:128 * (e["t"] + 1)].rearrange("h d k -> d h k")
                eng.dma_start(out=dst,
                              in_=stg[:, tr % 2, :].rearrange("p (h q) -> p h q", h=4)
                              ).then_inc(s_stg[tr % 2], 16)
                tr += 1
            eng.wait_ge(s_stg[0], 16 * 96)
            eng.wait_ge(s_stg[1], 16 * 96)
            eng.wait_ge(s_vst[0], 16 * 64)
            eng.wait_ge(s_vst[1], 16 * 64)

        phase(_pn='A1', gpsimd=a1_gpsimd, sync=a1_sync, tensor=a1_tensor, vector=a1_vector, scalar=a1_scalar)
        stA1.close()

        stA2 = contextlib.ExitStack()
        sb2 = lambda name, shape, dt: stA2.enter_context(nc.sbuf_tensor(name, shape, dt))
        wgbuf = sb2("wgbuf", [128, 2, DC, 256], BF16)
        wubuf = sb2("wubuf", [128, 2, DC, 256], BF16)
        gt = sb2("gt", [128, 2, 1024], BF16)
        htile = sb2("htile", [128, 2, 1024], BF16)
        psG = stA2.enter_context(nc.psum_tensor("psG", [128, 2, 1024], F32))
        psU = stA2.enter_context(nc.psum_tensor("psU", [128, 2, 1024], F32))

        s_w2 = dsem("s_w2", 2); s_gmm = sem("s_gmm"); s_umm = sem("s_umm")
        s_sil = sem("s_sil"); s_h2 = sem("s_h2"); s_hst = dsem("s_hst", 2)

        def a2_gpsimd(eng):
            rows = INTER // 16
            for bi in range(NBLK):
                if bi >= 2:
                    eng.wait_ge(s_umm, 2 * (bi - 1))
                p = bi % 2
                sg = wgT.rearrange("(c k) n -> k c n", k=128)
                su = wuT.rearrange("(c k) n -> k c n", k=128)
                eng.dma_start(out=wgbuf[:, p, :, :], in_=sg[:, :, 256 * bi:256 * (bi + 1)]).then_inc(s_w2[p], 16)
                eng.dma_start(out=wubuf[:, p, :, :], in_=su[:, :, 256 * bi:256 * (bi + 1)]).then_inc(s_w2[p], 16)
                if 2 <= bi < 18:
                    j = bi - 2
                    eng.dma_start(out=wd_bf[j * rows:(j + 1) * rows, :],
                                  in_=wdT[j * rows:(j + 1) * rows, :],
                                  max_dma_last_dim=8192).then_inc(s_wdc, 16)

        def a2_tensor(eng):
            for bi in range(NBLK):
                eng.wait_ge(s_w2[bi % 2], 32 * (bi // 2 + 1))
                for it in range(2):
                    ii = 2 * bi + it
                    pg = ii % 2
                    if ii >= 2:
                        eng.wait_ge(s_sil, ii - 1)
                        eng.wait_ge(s_h2, ii - 1)
                    for half in range(2):
                        for hc in range(DC):
                            mm = eng.matmul(psG[:, pg, 512 * half:512 * (half + 1)],
                                            wgbuf[:, bi % 2, hc, 128 * it:128 * (it + 1)],
                                            xnT[:, hc, 512 * half:512 * (half + 1)],
                                            start=(hc == 0), stop=(hc == DC - 1))
                    mm.then_inc(s_gmm, 1)
                    for half in range(2):
                        for hc in range(DC):
                            mm = eng.matmul(psU[:, pg, 512 * half:512 * (half + 1)],
                                            wubuf[:, bi % 2, hc, 128 * it:128 * (it + 1)],
                                            xnT[:, hc, 512 * half:512 * (half + 1)],
                                            start=(hc == 0), stop=(hc == DC - 1))
                    mm.then_inc(s_umm, 1)

        def a2_scalar(eng):
            for ii in range(IT):
                eng.wait_ge(s_gmm, ii + 1)
                if ii >= 2:
                    eng.wait_ge(s_h2, ii - 1)
                eng.activation(gt[:, ii % 2, :], psG[:, ii % 2, :], AF.Silu).then_inc(s_sil, 1)

        def a2_vector(eng):
            for ii in range(IT):
                eng.wait_ge(s_sil, ii + 1)
                eng.wait_ge(s_umm, ii + 1)
                if ii >= 2:
                    eng.wait_ge(s_hst[ii % 2], 16 * (ii // 2))
                eng.tensor_tensor(htile[:, ii % 2, :], gt[:, ii % 2, :], psU[:, ii % 2, :],
                                  ALU.mult).then_inc(s_h2, 1)

        def a2_sync(eng):
            for ii in range(IT):
                eng.wait_ge(s_h2, ii + 1)
                eng.dma_start(out=hT_d[ii], in_=htile[:, ii % 2, :]).then_inc(s_hst[ii % 2], 16)
            eng.wait_ge(s_hst[0], 16 * 43)
            eng.wait_ge(s_hst[1], 16 * 43)

        phase(_pn='A2', gpsimd=a2_gpsimd, tensor=a2_tensor, scalar=a2_scalar, vector=a2_vector, sync=a2_sync)
        stA2.close()
        stA.close()

        stC1 = contextlib.ExitStack()
        sbc = lambda name, shape, dt: stC1.enter_context(nc.sbuf_tensor(name, shape, dt))
        wdbuf = sbc("wdbuf", [128, 2, IT, 512], BF16)
        hbuf = sbc("hbuf", [128, 8, 1024], BF16)
        dstage = sbc("dstage", [128, 2, 512], F32)
        bo_sb = sbc("bo_sb", [128, D], BF16)
        psD = stC1.enter_context(nc.psum_tensor("psD", [128, 8, 512], F32))

        s_wd = dsem("s_wd", 2); s_hl = dsem("s_hl", 4); s_di = sem("s_di")
        s_dev = sem("s_dev"); s_dst = dsem("s_dst", 2); s_bo = sem("s_bo")

        WD_SPLIT = [(0, 22), (22, 22), (44, 21), (65, 21)]

        def c1_gpsimd(eng):
            eng.dma_start(out=bo_sb[:], in_=bo_b[:, :], max_dma_last_dim=8192).then_inc(s_bo, 16)
            eng.wait_ge(s_wdc, 16 * 16)
            for dout in range(8):
                if dout >= 2:
                    eng.wait_ge(s_di, 86 * (dout - 1))
                p = dout % 2
                src = wd_bf.rearrange("(c k) n -> k c n", k=128)
                for (c0, cn) in WD_SPLIT:
                    eng.dma_start(out=wdbuf[:, p, c0:c0 + cn, :],
                                  in_=src[:, c0:c0 + cn, 512 * dout:512 * (dout + 1)]).then_inc(s_wd[p], 16)

        def c1_sync(eng):
            for dout in range(8):
                for p2 in range(IT // 2):
                    gp = dout * (IT // 2) + p2
                    if gp >= 4:
                        k_last = (gp - 4) * 2 + 2
                        eng.wait_ge(s_di, k_last)
                    eng.dma_start(out=hbuf[:, (gp % 4) * 2:(gp % 4) * 2 + 2, :],
                                  in_=hT_d[2 * p2:2 * p2 + 2].rearrange("i p c -> p i c")
                                  ).then_inc(s_hl[gp % 4], 16)

        def c1_tensor(eng):
            for dout in range(8):
                eng.wait_ge(s_wd[dout % 2], 64 * (dout // 2 + 1))
                for i in range(IT):
                    gp = dout * (IT // 2) + i // 2
                    eng.wait_ge(s_hl[gp % 4], 16 * (gp // 4 + 1))
                    slot = (gp % 4) * 2 + (i % 2)
                    for q in range(8):
                        if i == 0 and dout >= 1:
                            eng.wait_ge(s_dev, 8 * (dout - 1) + q + 1)
                        mm = eng.matmul(psD[:, q, :], hbuf[:, slot, 128 * q:128 * (q + 1)],
                                        wdbuf[:, dout % 2, i, :],
                                        start=(i == 0), stop=(i == IT - 1))
                    mm.then_inc(s_di, 1)

        def c1_vector(eng):
            eng.wait_ge(s_bo, 16)
            for dout in range(8):
                for q in range(8):
                    k = dout * 8 + q
                    eng.wait_ge(s_di, 86 * (dout + 1))
                    if k >= 2:
                        eng.wait_ge(s_dst[k % 2], 16 * (k // 2))
                    eng.tensor_tensor(dstage[:, k % 2, :], psD[:, q, :],
                                      bo_sb[:, 512 * dout:512 * (dout + 1)], ALU.add).then_inc(s_dev, 1)

        def c1_scalar(eng):
            for dout in range(8):
                for q in range(8):
                    k = dout * 8 + q
                    eng.wait_ge(s_dev, k + 1)
                    eng.dma_start(out=dn_d[128 * q:128 * (q + 1), 512 * dout:512 * (dout + 1)],
                                  in_=dstage[:, k % 2, :]).then_inc(s_dst[k % 2], 16)
            eng.wait_ge(s_dst[0], 16 * 32)
            eng.wait_ge(s_dst[1], 16 * 32)

        phase(_pn='C1', gpsimd=c1_gpsimd, sync=c1_sync, tensor=c1_tensor, vector=c1_vector, scalar=c1_scalar)
        stC1.close()

        stBC = contextlib.ExitStack()
        oT = stBC.enter_context(nc.sbuf_tensor("oT", [128, H, 1024], BF16))
        wobuf = stBC.enter_context(nc.sbuf_tensor("wobuf", [128, 2, DC, 512], BF16))
        s_wo = dsem("s_wo", 2)

        stB = contextlib.ExitStack()
        sbb = lambda name, shape, dt: stB.enter_context(nc.sbuf_tensor(name, shape, dt))
        kbuf = sbb("kbuf", [128, 2, 2048], BF16)
        vbuf = sbb("vbuf", [128, 2, KT, 128], BF16)
        qbuf = sbb("qbuf", [128, 2, 1024], BF16)
        P = sbb("P", [128, 4, 2048], BF16)
        PT = sbb("PT", [128, KT, 1024], BF16)
        sumexp = sbb("sumexp", [128, 2, QT], F32)
        rs = sbb("rs", [128, QT], F32)
        bscr = sbb("bscr", [128, 1], F32)
        psS = stB.enter_context(nc.psum_tensor("psS", [128, 4, 512], F32))
        psO = stB.enter_context(nc.psum_tensor("psO", [128, 2, 512], F32))
        ptrB = stB.enter_context(nc.psum_tensor("ptrB", [128, 2, 1024], BF16))

        s_kl = dsem("s_kl", 2); s_smm = sem("s_smm"); s_expa = sem("s_expa"); s_expb = sem("s_expb")
        s_rc = sem("s_rc"); s_nrm = sem("s_nrm"); s_ptr = sem("s_ptr")
        s_pte0 = sem("s_pte0"); s_pte1 = sem("s_pte1")
        s_av = sem("s_av"); s_oev = sem("s_oev")

        def b_sync(eng):
            for h in range(H):
                if h >= 2:
                    eng.wait_ge(s_av, h - 1)
                p = h % 2
                eng.dma_start(out=kbuf[:, p, :], in_=kT_d[h]).then_inc(s_kl[p], 16)
                eng.dma_start(out=vbuf[:, p, :, :],
                              in_=v_d[h].rearrange("t k d -> k t d")).then_inc(s_kl[p], 16)
                eng.dma_start(out=qbuf[:, p, :], in_=qT_d[h]).then_inc(s_kl[p], 16)

        def b_tensor(eng):
            for h in range(H):
                eng.wait_ge(s_kl[h % 2], 48 * (h // 2 + 1))
                for qt in range(QT):
                    step = QT * h + qt
                    eng.wait_ge(s_expa, step)
                    for j in range(2):
                        mm = eng.matmul(psS[:, j, :], qbuf[:, h % 2, 128 * qt:128 * (qt + 1)],
                                        kbuf[:, h % 2, 512 * j:512 * (j + 1)],
                                        start=True, stop=True)
                    mm.then_inc(s_smm, 1)
                    eng.wait_ge(s_expb, step)
                    for j in range(2, 4):
                        mm = eng.matmul(psS[:, j, :], qbuf[:, h % 2, 128 * qt:128 * (qt + 1)],
                                        kbuf[:, h % 2, 512 * j:512 * (j + 1)],
                                        start=True, stop=True)
                    mm.then_inc(s_smm, 1)
                    if qt >= 2:
                        emit_ptr(eng, h, qt - 2)
                emit_ptr(eng, h, QT - 2)
                emit_ptr(eng, h, QT - 1)
                eng.wait_ge(s_pte0, 8 * (h + 1))
                eng.wait_ge(s_pte1, 8 * (h + 1))
                if h >= 1:
                    eng.wait_ge(s_oev, h)
                for kc in range(KT):
                    eng.matmul(psO[:, 0, :], vbuf[:, h % 2, kc, :], PT[:, kc, 0:512],
                               start=(kc == 0), stop=(kc == KT - 1))
                    mm = eng.matmul(psO[:, 1, :], vbuf[:, h % 2, kc, :], PT[:, kc, 512:1024],
                                    start=(kc == 0), stop=(kc == KT - 1))
                mm.then_inc(s_av, 1)

        def emit_ptr(eng, h, qtx):
            step = QT * h + qtx
            eng.wait_ge(s_nrm, step + 1)
            for grp in range(2):
                gi = 2 * step + grp
                if gi >= 2:
                    eng.wait_ge(s_pte0 if grp == 0 else s_pte1, gi // 2)
                for c in range(8):
                    kc = 8 * grp + c
                    tr = eng.transpose(ptrB[:, grp, 128 * c:128 * (c + 1)],
                                       P[:, qtx % 4, 128 * kc:128 * (kc + 1)], ident[:])
                tr.then_inc(s_ptr, 1)

        def b_evac(eng, h, qtx):
            gi = 2 * (QT * h + qtx)
            eng.wait_ge(s_ptr, gi + 1)
            if h >= 1:
                eng.wait_ge(s_av, h)
            eng.tensor_copy(PT[:, 0:8, 128 * qtx:128 * (qtx + 1)],
                            ptrB[:, 0, :].rearrange("p (c q) -> p c q", c=8)).then_inc(s_pte0, 1)
            eng.wait_ge(s_ptr, gi + 2)
            eng.tensor_copy(PT[:, 8:16, 128 * qtx:128 * (qtx + 1)],
                            ptrB[:, 1, :].rearrange("p (c q) -> p c q", c=8)).then_inc(s_pte1, 1)

        def b_vector(eng):
            for h in range(H):
                for qt in range(QT):
                    step = QT * h + qt
                    eng.wait_ge(s_expa, step + 1)
                    eng.wait_ge(s_expb, step + 1)
                    if h >= 1:
                        eng.wait_ge(s_nrm, QT * (h - 1) + qt + 1)
                    tse = eng.tensor_scalar(rs[:, qt:qt + 1], sumexp[:, 0, qt:qt + 1],
                                            sumexp[:, 1, qt:qt + 1], None, ALU.add)
                    ch_dve.done(tse)
                    ch_dve.wait(eng)
                    eng.reciprocal(rs[:, qt:qt + 1], rs[:, qt:qt + 1]).then_inc(s_rc, 1)
                    if qt >= 2:
                        b_evac(eng, h, qt - 2)
                b_evac(eng, h, QT - 2)
                b_evac(eng, h, QT - 1)
                eng.wait_ge(s_av, h + 1)
                eng.tensor_copy(oT[:, h, :], psO[:, :, :].rearrange("p a b -> p (a b)")).then_inc(s_oev, 1)

        def b_scalar(eng):
            for h in range(H):
                for qt in range(QT):
                    step = QT * h + qt
                    eng.wait_ge(s_smm, 2 * step + 1)
                    if step >= 4:
                        eng.wait_ge(s_ptr, 2 * (step - 4) + 2)
                    if h >= 1:
                        eng.wait_ge(s_rc, QT * (h - 1) + qt + 1)
                    eng.activation(P[:, qt % 4, 0:1024],
                                   psS[:, 0:2, :].rearrange("p a b -> p (a b)"),
                                   AF.Exp, bias=negb[:, 0:1], scale=1.0,
                                   accum_out=sumexp[:, 0, qt:qt + 1]).then_inc(s_expa, 1)
                    eng.wait_ge(s_smm, 2 * step + 2)
                    eng.activation(P[:, qt % 4, 1024:2048],
                                   psS[:, 2:4, :].rearrange("p a b -> p (a b)"),
                                   AF.Exp, bias=negb[:, 0:1], scale=1.0,
                                   accum_out=sumexp[:, 1, qt:qt + 1]).then_inc(s_expb, 1)

        def b_gpsimd(eng):
            for dout in range(2):
                srcw = wo.rearrange("(c k) n -> k c n", k=128)
                eng.dma_start(out=wobuf[:, dout % 2, :, :],
                              in_=srcw[:, :, 512 * dout:512 * (dout + 1)]).then_inc(s_wo[dout % 2], 16)
            for h in range(H):
                for qt in range(QT):
                    step = QT * h + qt
                    eng.wait_ge(s_expa, step + 1)
                    eng.wait_ge(s_expb, step + 1)
                    eng.wait_ge(s_rc, step + 1)
                    eng.tensor_scalar(P[:, qt % 4, :], P[:, qt % 4, :], rs[:, qt:qt + 1], 1.0,
                                      ALU.mult, ALU.mult).then_inc(s_nrm, 1)

        phase(_pn='B', sync=b_sync, tensor=b_tensor, vector=b_vector, scalar=b_scalar, gpsimd=b_gpsimd)
        stB.close()

        stC2 = contextlib.ExitStack()
        sbo = lambda name, shape, dt: stC2.enter_context(nc.sbuf_tensor(name, shape, dt))
        xbufc = sbo("xbufc", [128, 4, 512], F32)
        dnbufc = sbo("dnbufc", [128, 4, 512], F32)
        tstage = sbo("tstage", [128, 2, 512], F32)
        ostage = sbo("ostage", [128, 2, 512], F32)
        psC2 = stC2.enter_context(nc.psum_tensor("psC2", [128, 8, 512], F32))

        s_xl = dsem("s_xl", 4); s_fmm = sem("s_fmm")
        s_fev = sem("s_fev"); s_f1 = sem("s_f1"); s_out = dsem("s_out", 2)

        def c2_gpsimd(eng):
            for dout in range(2, 8):
                if dout >= 2:
                    eng.wait_ge(s_fmm, 8 * (dout - 1))
                src = wo.rearrange("(c k) n -> k c n", k=128)
                eng.dma_start(out=wobuf[:, dout % 2, :, :],
                              in_=src[:, :, 512 * dout:512 * (dout + 1)]).then_inc(s_wo[dout % 2], 16)

        def c2_sync(eng):
            for dout in range(8):
                for q in range(8):
                    idx = 8 * dout + q
                    if idx >= 4:
                        eng.wait_ge(s_f1, idx - 3)
                    rsl = slice(128 * q, 128 * (q + 1))
                    csl = slice(512 * dout, 512 * (dout + 1))
                    eng.dma_start(out=xbufc[:, idx % 4, :], in_=x_own[rsl, csl]).then_inc(s_xl[idx % 4], 16)
                    eng.dma_start(out=dnbufc[:, idx % 4, :], in_=dn_d[rsl, csl]).then_inc(s_xl[idx % 4], 16)
            eng.wait_ge(s_out[0], 16 * 32)
            eng.wait_ge(s_out[1], 16 * 32)

        def c2_tensor(eng):
            for dout in range(8):
                eng.wait_ge(s_wo[dout % 2], 16 * (dout // 2 + 1))
                for q in range(8):
                    idx = 8 * dout + q
                    if idx >= 8:
                        eng.wait_ge(s_fev, idx - 7)
                    for dc in range(DC):
                        mm = eng.matmul(psC2[:, q, :], oT[:, dc, 128 * q:128 * (q + 1)],
                                        wobuf[:, dout % 2, dc, :],
                                        start=(dc == 0), stop=(dc == DC - 1))
                    mm.then_inc(s_fmm, 1)

        def c2_vector(eng):
            for dout in range(8):
                for q in range(8):
                    idx = 8 * dout + q
                    eng.wait_ge(s_fmm, idx + 1)
                    eng.wait_ge(s_xl[idx % 4], 32 * (idx // 4 + 1))
                    if idx >= 2:
                        eng.wait_ge(s_out[idx % 2], 16 * (idx // 2))
                    eng.tensor_tensor(tstage[:, idx % 2, :], psC2[:, q, :],
                                      xbufc[:, idx % 4, :], ALU.add).then_inc(s_fev, 1)
                    eng.wait_ge(s_fev, idx + 1)
                    eng.tensor_tensor(ostage[:, idx % 2, :], tstage[:, idx % 2, :],
                                      dnbufc[:, idx % 4, :], ALU.add).then_inc(s_f1, 1)

        def c2_scalar(eng):
            for dout in range(8):
                for q in range(8):
                    idx = 8 * dout + q
                    eng.wait_ge(s_f1, idx + 1)
                    eng.dma_start(out=out[128 * q:128 * (q + 1), 512 * dout:512 * (dout + 1)],
                                  in_=ostage[:, idx % 2, :]).then_inc(s_out[idx % 2], 16)
            eng.wait_ge(s_out[0], 16 * 32)
            eng.wait_ge(s_out[1], 16 * 32)

        phase(_pn='C2', gpsimd=c2_gpsimd, sync=c2_sync, tensor=c2_tensor, vector=c2_vector, scalar=c2_scalar)
        stC2.close()
        stBC.close()

    return nc


def kernel(**inp):
    x = np.ascontiguousarray(np.asarray(inp["L0_attn_hidden_states"], dtype=np.float32))
    ident = np.eye(128, dtype=ml_dtypes.bfloat16)
    f32c = lambda a: np.ascontiguousarray(np.asarray(a, dtype=np.float32))
    common = {
        "wq": f32c(inp["L0_attn_Wq"]), "wk": f32c(inp["L0_attn_Wk"]),
        "wv": f32c(inp["L0_attn_Wv"]), "wo": f32c(inp["L0_attn_Wo"]),
        "wgT": f32c(np.asarray(inp["L0_ffn_w_gate"]).T),
        "wuT": f32c(np.asarray(inp["L0_ffn_w_up"]).T),
        "wdT": f32c(np.asarray(inp["L0_ffn_w_down"]).T),
        "bq_b": f32c(np.broadcast_to(inp["L0_attn_bq"], (128, D))),
        "bk_b": f32c(np.broadcast_to(inp["L0_attn_bk"], (128, D))),
        "bv_b": f32c(np.broadcast_to(inp["L0_attn_bv"], (128, D))),
        "bo_b": f32c(np.broadcast_to(inp["L0_attn_bo"], (128, D))),
        "gq_b": f32c(np.broadcast_to(inp["L0_attn_gq"], (128, HD))),
        "gk_b": f32c(np.broadcast_to(inp["L0_attn_gk"], (128, HD))),
        "ident": ident,
    }
    in_maps = []
    for c in range(8):
        b, half = c // 2, c % 2
        xb = x[b]
        m = dict(common)
        m["x_own"] = np.ascontiguousarray(xb[half * 1024:(half + 1) * 1024])
        m["x_oth"] = np.ascontiguousarray(xb[(1 - half) * 1024:(2 - half) * 1024])
        in_maps.append(m)
    nc = build()
    res = run_bass_kernel_spmd(nc, in_maps, core_ids=list(range(8)))
    outp = np.empty((B, N, D), np.float32)
    for c, r in enumerate(res.results):
        b, half = c // 2, c % 2
        outp[b, half * 1024:(half + 1) * 1024] = r["out"]
    return outp
